# version 20
# baseline (speedup 1.0000x reference)
import numpy as np
import ml_dtypes
import concourse.bass as bass
import concourse.mybir as mybir
from concourse.bass_utils import run_bass_kernel_spmd
from contextlib import ExitStack

F32, BF16 = mybir.dt.float32, mybir.dt.bfloat16
AF, ALU = mybir.ActivationFunctionType, mybir.AluOpType
bfd = ml_dtypes.bfloat16

T, D, NH, DH = 2048, 1024, 16, 64
NT = T // 128
NC = D // 128
QG = 512
NQG = T // QG
TCQ = QG // 128
EPS = 1e-5

def host_constants():
    j = np.arange(D)
    ang = 2.0 * np.pi * np.outer(j, j) / D
    Cp = (np.cos(ang) / 32.0).astype(np.float32)
    Sp = (np.sin(ang) / 32.0).astype(np.float32)
    inv = 1.0 / 10000.0 ** (np.arange(0, DH, 2, dtype=np.float32) / DH)
    freqs = np.arange(T, dtype=np.float32)[:, None] * inv[None, :]
    cosT = np.cos(freqs); sinT = np.sin(freqs)
    cosTt = np.ascontiguousarray(cosT.reshape(NT, 128, 32).transpose(1, 0, 2).reshape(128, NT * 32))
    sinTt = np.ascontiguousarray(sinT.reshape(NT, 128, 32).transpose(1, 0, 2).reshape(128, NT * 32))
    ones128 = np.ones((1, 128), dtype=bfd)
    ident64 = np.eye(64, dtype=bfd)
    return Cp, Sp, cosTt, sinTt, ones128, ident64

def perm_f_of_pos():
    p = np.arange(D); head = p // DH; r = p % DH
    return np.where(r < 32, head * DH + 2 * r, head * DH + 2 * (r - 32) + 1)

class Sem:
    __slots__ = ("name", "n", "h")
    def __init__(s, name): s.name, s.n, s.h = name, 0, None

def build(debug=()):
    nc = bass.Bass("TRN2", target_bir_lowering=False)
    H = lambda n, shp, dt=F32, kind="ExternalInput": nc.dram_tensor(n, shp, dt, kind=kind).ap()
    x_h   = H("x", [T, D])
    wq_h  = H("wqpt", [D, D]); wk_h = H("wkpt", [D, D])
    wv_h  = H("wvt", [D, D]);  wo_h = H("wot", [D, D])
    cp_h  = H("cp", [D, D]);   sp_h = H("sp", [D, D])
    wrt_h = H("wrt", [D, D]);  wit_h = H("wit", [D, D])
    bias_h = H("biasrow", [1, D])
    cosT_h = H("cost", [128, NT * 32]); sinT_h = H("sint", [128, NT * 32])
    ones_h = H("ones128", [1, 128], BF16)
    id64_h = H("ident64", [64, 64], BF16)
    out_h = H("out", [T, D], F32, kind="ExternalOutput")
    dbg = {k: H("dbg_" + k, shp, F32, kind="ExternalOutput") for k, shp in {
        "m":   [128, NC * 1024],
        "ht":  [128, NC * T],
        "kt":  [128, NC * T],
        "v65": [128, NT * NH * 65],
        "qt":  [128, NC * QG],
        "ot":  [128, NC * QG],
        "x1":  [128, TCQ * 1024],
    }.items() if k in debug}

    es = ExitStack(); E = es.enter_context
    U0 = E(nc.sbuf_tensor("U0", [128, 16384], BF16))
    U1 = E(nc.sbuf_tensor("U1", [128, 16384], BF16))
    U2 = E(nc.sbuf_tensor("U2", [128, 16640], BF16))
    Msb = E(nc.sbuf_tensor("Msb", [128, 8192], BF16))
    W0 = E(nc.sbuf_tensor("W0", [128, 8192], BF16))
    W1 = E(nc.sbuf_tensor("W1", [128, 8192], BF16))
    xs  = E(nc.sbuf_tensor("xs", [128, 2 * 1024], F32))
    hs  = E(nc.sbuf_tensor("hs", [128, 2 * 1024], BF16))
    rop = E(nc.sbuf_tensor("rop", [128, 2 * 1024], BF16))
    rsc = E(nc.sbuf_tensor("rsc", [128, 512], BF16))
    PT  = E(nc.sbuf_tensor("PT", [128, 2 * 1024], BF16))
    cosb = E(nc.sbuf_tensor("cosb", [128, NT * 32], BF16))
    sinb = E(nc.sbuf_tensor("sinb", [128, NT * 32], BF16))
    onesb = E(nc.sbuf_tensor("onesb", [1, 128], BF16))
    id64b = E(nc.sbuf_tensor("id64b", [64, 64], BF16))
    biasb = E(nc.sbuf_tensor("biasb", [1, D], BF16))
    qTg = E(nc.sbuf_tensor("qTg", [128, NC * QG], BF16))
    OTg = E(nc.sbuf_tensor("OTg", [128, NC * QG], BF16))
    recl = E(nc.sbuf_tensor("recl", [1, 2 * QG], BF16))
    x1g = E(nc.sbuf_tensor("x1g", [128, TCQ * 1024], F32))
    lnbuf = x1g[0:1, 3584:4096]
    tmpg = hs[:, 0:1024]
    h2s = hs[:, 1024:2048]

    bst = E(nc.sbuf_tensor("bst", [128, 4 * 12], F32))
    agA = E(nc.sbuf_tensor("agA", [128, 4 * 2], F32))
    agB = E(nc.sbuf_tensor("agB", [128, 4 * 2], F32))
    stt = E(nc.sbuf_tensor("stt", [128, 4 * 10], F32))
    epsb = E(nc.sbuf_tensor("epsb", [128, 1], F32))
    halfb = E(nc.sbuf_tensor("halfb", [128, 1], F32))
    nhalfb = E(nc.sbuf_tensor("nhalfb", [128, 1], F32))
    repsb = E(nc.sbuf_tensor("repsb", [128, QG], BF16))
    h2TgT = E(nc.sbuf_tensor("h2TgT", [128, 1024], BF16))

    psA = E(nc.psum_tensor("psA", [128, 2048], F32))
    psB = E(nc.psum_tensor("psB", [128, 2048], F32))
    ps0 = psA[:, 0:1024]
    ps1 = psA[:, 1024:2048]
    ps2 = psB[:, 0:1024]
    ps3 = psB[:, 1024:2048]
    ROLE = {0: (ps0, ps1, ps2, ps3)}
    for _q in range(1, NQG):
        _s0, _s1, _p0, _p1 = ROLE[_q - 1]
        ROLE[_q] = (_p1, _p0, _s1, _s0)

    sD = Sem("sD"); sG = Sem("sG"); sT = Sem("sT"); sV = Sem("sV"); sS = Sem("sS")
    sP = Sem("sP"); sX = Sem("sX"); sH = Sem("sH")
    for s in (sD, sG, sT, sV, sS, sP, sX, sH):
        s.h = E(nc.semaphore(s.name))

    ops = {k: [] for k in ("sync", "gpsimd", "tensor", "vector", "scalar")}
    def op(eng, emit, waits=(), inc=None, inc_by=1):
        wl = tuple(w for w in waits if w is not None)
        def run(e, emit=emit, wl=wl, inc=inc, inc_by=inc_by):
            for (s, v) in wl:
                e.wait_ge(s.h, v)
            ins = emit(e)
            if inc is not None:
                ins.then_inc(inc.h, inc_by)
        ops[eng].append(run)
        if inc is not None:
            inc.n += inc_by
    def ev(s): return (s, s.n)

    op("gpsimd", lambda e: e.dma_start(out=cosb[:], in_=cosT_h[:]), inc=sG, inc_by=16)
    op("gpsimd", lambda e: e.dma_start(out=sinb[:], in_=sinT_h[:]), inc=sG, inc_by=16)
    op("sync", lambda e: e.dma_start(out=onesb[:], in_=ones_h[:]), inc=sD, inc_by=16)
    op("sync", lambda e: e.dma_start(out=id64b[:], in_=id64_h[:]), inc=sD, inc_by=16)
    ev_id64 = ev(sD)
    def wload(dstview, src):
        i3 = src.rearrange("(r p) c -> p r c", p=128)
        o3 = dstview.rearrange("p (r c) -> p r c", c=1024)
        return lambda e, i3=i3, o3=o3: e.dma_start(out=o3, in_=i3)
    op("gpsimd", wload(U0[:, 0:8192], cp_h), inc=sG, inc_by=16);     ev_cp = ev(sG)
    op("gpsimd", wload(U0[:, 8192:16384], sp_h), inc=sG, inc_by=16); ev_sp = ev(sG)
    op("gpsimd", wload(U1[:, 0:8192], wrt_h), inc=sG, inc_by=16);    ev_wr = ev(sG)
    op("gpsimd", wload(U1[:, 8192:16384], wit_h), inc=sG, inc_by=16); ev_wi = ev(sG)
    op("gpsimd", wload(W0[:, 0:8192], wk_h), inc=sG, inc_by=16);     ev_wk = ev(sG)
    op("gpsimd", wload(W1[:, 0:8192], wv_h), inc=sG, inc_by=16);     ev_wv = ev(sG)
    op("gpsimd", lambda e: e.dma_start(out=biasb[:], in_=bias_h[:]), inc=sG, inc_by=16); ev_bias = ev(sG)

    Cp_v = U0[:, 0:8192]; Sp_v = U0[:, 8192:16384]
    WrT_v = U1[:, 0:8192]; WiT_v = U1[:, 8192:16384]
    Br_v = U2[:, 0:8192]; Bi_v = U2[:, 8192:16384]

    def phase0_product(dst_view, pairs, first_waits):
        evacs = []
        idx = 0
        for mc in range(8):
            for ng in range(2):
                ps = ps0 if (idx % 2 == 0) else ps1
                pss = ps[:, 0:512]
                n0 = ng * 512
                nmm = len(pairs) * 8
                i = 0
                for (lv, rv) in pairs:
                    for kc in range(8):
                        lhsT = lv[:, 1024 * kc + 128 * mc: 1024 * kc + 128 * mc + 128]
                        rhs = rv[:, 1024 * kc + n0: 1024 * kc + n0 + 512]
                        w = []
                        if i == 0:
                            if idx == 0:
                                w = list(first_waits)
                            if idx >= 2:
                                w.append(evacs[idx - 2])
                        last = (i == nmm - 1)
                        op("tensor", (lambda e, o=pss, l=lhsT, r=rhs, st=(i == 0), sp=last:
                                      e.matmul(o, l, r, start=st, stop=sp)),
                           waits=tuple(w), inc=(sT if last else None))
                        i += 1
                evT = ev(sT)
                dstv = dst_view[:, 1024 * mc + n0: 1024 * mc + n0 + 512]
                op("vector", (lambda e, o=dstv, i2=pss: e.tensor_copy(o, i2)), waits=(evT,), inc=sV)
                evacs.append(ev(sV))
                idx += 1
        return evacs

    phase0_product(Bi_v, [(WrT_v, Sp_v), (WiT_v, Cp_v)], (ev_cp, ev_sp, ev_wr, ev_wi))
    op("vector", lambda e: e.tensor_scalar(WiT_v, WiT_v, -1.0, None, ALU.mult), inc=sV)
    ev_win = ev(sV)
    phase0_product(Br_v, [(WrT_v, Cp_v), (WiT_v, Sp_v)], (ev_win,))
    ev_br_done = ev(sV)
    mevacs = phase0_product(Msb[:, 0:8192], [(Cp_v, Br_v), (Sp_v, Bi_v)], (ev_br_done,))
    ev_mT_last = ev(sT)
    ev_m_done = ev(sV)
    if "m" in dbg:
        op("gpsimd", lambda e: e.dma_start(out=dbg["m"][:], in_=Msb[:]), waits=(ev_m_done,), inc=sX, inc_by=16)

    op("vector", lambda e: e.memset(epsb[:], EPS))
    op("vector", lambda e: e.memset(halfb[:], 0.5))
    op("vector", lambda e: e.memset(nhalfb[:], -0.5), inc=sV)
    last_comb = [None, None, None, None]
    ln_ctr = [0]
    def ln_stats2(srcA, srcB, waits):
        sl = ln_ctr[0] % 4; ln_ctr[0] += 1
        b0 = bst[:, 12 * sl:12 * sl + 6]; b1 = bst[:, 12 * sl + 6:12 * sl + 12]
        aA = agA[:, 2 * sl:2 * sl + 2]; aB = agB[:, 2 * sl:2 * sl + 2]
        s_ = stt[:, 10 * sl:10 * sl + 10]
        w = list(waits) + ([last_comb[sl]] if last_comb[sl] else [])
        op("vector", lambda e: e.bn_stats(b0, srcA), waits=tuple(w))
        op("vector", lambda e: e.bn_stats(b1, srcB))
        op("vector", lambda e: e.bn_aggr(aA, b0))
        op("vector", lambda e: e.bn_aggr(aB, b1), inc=sV)
        evagg = ev(sV)
        op("gpsimd", lambda e: e.tensor_tensor(s_[:, 0:1], aA[:, 0:1], aB[:, 0:1], ALU.add), waits=(evagg,))
        op("gpsimd", lambda e: e.tensor_tensor(s_[:, 1:2], aA[:, 0:1], aB[:, 0:1], ALU.subtract))
        op("gpsimd", lambda e: e.tensor_tensor(s_[:, 2:3], aA[:, 1:2], aB[:, 1:2], ALU.add))
        op("gpsimd", lambda e: e.tensor_tensor(s_[:, 3:4], s_[:, 1:2], s_[:, 1:2], ALU.mult))
        op("gpsimd", lambda e: e.tensor_tensor(s_[:, 3:4], s_[:, 3:4], halfb[:], ALU.mult))
        op("gpsimd", lambda e: e.tensor_tensor(s_[:, 4:5], s_[:, 3:4], s_[:, 2:3], ALU.add))
        op("gpsimd", lambda e: e.tensor_tensor(s_[:, 6:7], s_[:, 0:1], nhalfb[:], ALU.mult), inc=sP)
        evcomb = ev(sP)
        last_comb[sl] = evcomb
        op("scalar", lambda e: e.activation(s_[:, 5:6], s_[:, 4:5], AF.Ln, bias=epsb[:], scale=0.5),
           waits=(evcomb,))
        op("scalar", lambda e: e.activation(s_[:, 7:8], s_[:, 5:6], AF.Exp, scale=-0.5), inc=sS)
        evsq = ev(sS)
        op("gpsimd", lambda e: e.tensor_tensor(s_[:, 8:9], s_[:, 6:7], s_[:, 7:8], ALU.mult), waits=(evsq,), inc=sP)
        return ev(sP), s_[:, 6:7], s_[:, 7:8], s_[:, 8:9]
    def ln_stats(src_view, waits):
        return ln_stats2(src_view[:, 0:512], src_view[:, 512:1024], waits)

    def rope_tile(ps_view, out_tile, g, waits):
        ct = cosb[:, 32 * g:32 * g + 32].rearrange("p (a b) -> p a b", a=1).to_broadcast((128, 16, 32))
        st = sinb[:, 32 * g:32 * g + 32].rearrange("p (a b) -> p a b", a=1).to_broadcast((128, 16, 32))
        p4 = ps_view.rearrange("p (h two j) -> p h two j", two=2, j=32)
        o4 = out_tile.rearrange("p (h two j) -> p h two j", two=2, j=32)
        e_in, o_in = p4[:, :, 0:1, :], p4[:, :, 1:2, :]
        e_out, o_out = o4[:, :, 0:1, :], o4[:, :, 1:2, :]
        sc1 = rsc[:].rearrange("p (h j) -> p h j", j=32)
        op("vector", lambda e: e.tensor_tensor(sc1, o_in, st, ALU.mult), waits=waits)
        op("vector", lambda e: e.tensor_tensor(e_out, e_in, ct, ALU.mult))
        op("vector", lambda e: e.tensor_tensor(e_out, e_out, sc1, ALU.subtract))
        op("vector", lambda e: e.tensor_tensor(sc1, o_in, ct, ALU.mult))
        op("vector", lambda e: e.tensor_tensor(o_out, e_in, st, ALU.mult))
        op("vector", lambda e: e.tensor_tensor(o_out, o_out, sc1, ALU.add), inc=sV)
        return ev(sV)

    def transpose_tile(dst, chunk_len, col_off, src_tile, waits):
        return transpose_tile_ap(dst[:], chunk_len, col_off, src_tile, waits)
    def transpose_tile_ap(dstap, chunk_len, col_off, src_ap, waits):
        outap = dstap.rearrange("p (c t) -> p c t", t=chunk_len)[:, :, col_off:col_off + 128]
        inap = src_ap.rearrange("p (c j) -> p c j", j=128)
        op("sync", (lambda e, o=outap, i=inap: e.dma_start(out=o, in_=i, transpose=True)),
           waits=waits, inc=sD, inc_by=16)
        return ev(sD)

    op("vector", lambda e: e.memset(U2[:], 1.0), inc=sV)
    ev_v65init = ev(sV)
    ev_hT = [None] * NT; ev_kT = [None] * NT; ev_vev = [None] * NT
    ev_h = [None] * NT; ev_rope1 = [None] * NT
    ev_x1load = [None] * NT
    def ph1_xload(g):
        xv = xs[:, 1024 * (g % 2):1024 * (g % 2) + 1024]
        wl = [(sH, g - 1)] if g >= 2 else []
        op("sync", (lambda e, o=xv, i=x_h.rearrange("(g p) c -> g p c", p=128)[g]:
                    e.dma_start(out=o, in_=i)), waits=tuple(wl), inc=sD, inc_by=16)
        ev_x1load[g] = ev(sD)
    def ph1_hoist(g):
        xv = xs[:, 1024 * (g % 2):1024 * (g % 2) + 1024]
        evr, negm, rstd, nmr = ln_stats(xv, (ev_x1load[g],))
        hv = hs[:, 1024 * (g % 2):1024 * (g % 2) + 1024]
        wl = [evr] + ([ev_hT[g - 2]] if g >= 2 else [])
        op("scalar", (lambda e, o=hv, i=xv, b=nmr, rs=rstd: e.activation(o, i, AF.Identity, bias=b, scale=rs)),
           waits=tuple(wl), inc=sH)
        op("scalar", lambda e: e.nop(), inc=sV)
        ev_h[g] = ev(sV)
        wl = [ev_h[g]] + ([ev_mT_last] if g == 0 else [])
        ev_hT[g] = transpose_tile(U0, T, 128 * g, hv, tuple(wl))
        if g + 4 < NT:
            ph1_xload(g + 4)
    for g in range(4):
        ph1_xload(g)
    ph1_hoist(0); ph1_hoist(1)
    for g in range(NT):
        if g + 2 < NT:
            ph1_hoist(g + 2)
        psK = ps0 if (g % 2 == 0) else ps2
        for ng in range(2):
            pss = psK[:, 512 * ng:512 * ng + 512]
            for kc in range(NC):
                lhsT = U0[:, T * kc + 128 * g: T * kc + 128 * g + 128]
                rhs = W0[:, 1024 * kc + 512 * ng: 1024 * kc + 512 * ng + 512]
                w = []
                if kc == 0 and ng == 0:
                    w = [ev_hT[g], ev_wk] + ([ev_rope1[g - 2]] if g >= 2 else [])
                op("tensor", (lambda e, o=pss, l=lhsT, r=rhs, st=(kc == 0), sp=(kc == NC - 1):
                              e.matmul(o, l, r, start=st, stop=sp)),
                   waits=tuple(w), inc=(sT if (kc == NC - 1 and ng == 1) else None))
        ev_kproj = ev(sT)
        if g == NT - 1:
            ev_kproj15 = ev_kproj
        rv = rop[:, 1024 * (g % 2):1024 * (g % 2) + 1024]
        wl = [ev_kproj] + ([ev_kT[g - 2]] if g >= 2 else [])
        ev_rope1[g] = rope_tile(psK[:, 0:1024], rv, g, tuple(wl))
        ev_kT[g] = transpose_tile(U1, T, 128 * g, rv, (ev_rope1[g],) + ((ev_mT_last,) if g == 0 else ()))
        psV = ps1 if (g % 2 == 0) else ps3
        for ng in range(2):
            pss = psV[:, 512 * ng:512 * ng + 512]
            for kc in range(NC):
                lhsT = U0[:, T * kc + 128 * g: T * kc + 128 * g + 128]
                rhs = W1[:, 1024 * kc + 512 * ng: 1024 * kc + 512 * ng + 512]
                w = []
                if kc == 0 and ng == 0:
                    w = [ev_hT[g], ev_wv, ev_v65init] + ([ev_vev[g - 2]] if g >= 2 else [])
                op("tensor", (lambda e, o=pss, l=lhsT, r=rhs, st=(kc == 0), sp=(kc == NC - 1):
                              e.matmul(o, l, r, start=st, stop=sp)),
                   waits=tuple(w), inc=(sT if (kc == NC - 1 and ng == 1) else None))
        ev_vproj = ev(sT)
        vdst = U2[:, 1040 * g:1040 * g + 1040].rearrange("p (h c) -> p h c", c=65)[:, :, 0:64]
        vsrc = psV[:, 0:1024].rearrange("p (h c) -> p h c", c=64)
        op("vector", (lambda e, o=vdst, i=vsrc: e.tensor_copy(o, i)), waits=(ev_vproj,), inc=sV)
        ev_vev[g] = ev(sV)
    ev_ph1_v = ev(sV)
    ev_ph1_kT = ev_kT[NT - 1]
    ev_ph1_T = ev(sT)
    if "ht" in dbg:
        op("gpsimd", lambda e: e.dma_start(out=dbg["ht"][:], in_=U0[:]), waits=(ev_hT[NT - 1],), inc=sX, inc_by=16)
    if "kt" in dbg:
        op("gpsimd", lambda e: e.dma_start(out=dbg["kt"][:], in_=U1[:]), waits=(ev_ph1_kT,), inc=sX, inc_by=16)
    if "v65" in dbg:
        op("gpsimd", lambda e: e.dma_start(out=dbg["v65"][:], in_=U2[:]), waits=(ev_ph1_v,), inc=sX, inc_by=16)

    op("gpsimd", wload(W0[:, 0:8192], wq_h), waits=(ev_kproj15,), inc=sG, inc_by=16)
    ev_wq = ev(sG)
    op("gpsimd", wload(W1[:, 0:8192], wo_h), waits=(ev_ph1_T,), inc=sG, inc_by=16)
    ev_wo = ev(sG)

    ev_exp = {}
    ev_scores_done_prev = None
    ev_oproj_last_prev = None
    ev_x1_done = [None] * TCQ
    ev_x1_last_prev = None
    ev_out_store_prev = None
    ev_ffn_prev = None
    qp_rope = {}; qp_qT = {}; qp_proj = {}
    def emit_qp_proj(qgx, tc, prev_ot_done):
        if qgx not in qp_rope:
            qp_rope[qgx] = [None] * TCQ; qp_qT[qgx] = [None] * TCQ; qp_proj[qgx] = [None] * TCQ
        g = qgx * TCQ + tc
        if qgx == 0:
            psf = (ps2, ps3, ps0, ps1)[tc]
        else:
            psf = ROLE[qgx - 1][3]
        for ng in range(2):
            pss = psf[:, 512 * ng:512 * ng + 512]
            for kc in range(NC):
                lhsT = U0[:, T * kc + 128 * g: T * kc + 128 * g + 128]
                rhs = W0[:, 1024 * kc + 512 * ng: 1024 * kc + 512 * ng + 512]
                w = []
                if kc == 0 and ng == 0:
                    w = [ev_wq]
                    if tc >= 2:
                        w.append(qp_rope[qgx][tc - 2])
                    if qgx == 0 and tc == 0:
                        w.append(ev_rope1[NT - 1])
                    if qgx == 0 and tc == 1:
                        w.append(ev_vev[NT - 1])
                    if qgx > 0:
                        w.append(prev_ot_done if tc == 0 else qp_rope[qgx][tc - 1])
                op("tensor", (lambda e, o=pss, l=lhsT, r=rhs, st=(kc == 0), sp=(kc == NC - 1):
                              e.matmul(o, l, r, start=st, stop=sp)),
                   waits=tuple(w), inc=(sT if (kc == NC - 1 and ng == 1) else None))
        qp_proj[qgx][tc] = ev(sT)
    def emit_qp_rope(qgx, tc):
        g = qgx * TCQ + tc
        psf = (ps2, ps3, ps0, ps1)[tc] if qgx == 0 else ROLE[qgx - 1][3]
        rv = rop[:, 1024 * (tc % 2):1024 * (tc % 2) + 1024]
        wl = [qp_proj[qgx][tc]] + ([qp_qT[qgx][tc - 2]] if tc >= 2 else [])
        if tc < 2:
            wl.append(ev_kT[NT - 2 + tc] if qgx == 0 else qp_qT[qgx - 1][TCQ - 2 + tc])
        qp_rope[qgx][tc] = rope_tile(psf[:, 0:1024], rv, g, tuple(wl))
        wl = [qp_rope[qgx][tc]] + ([ev_scores_done_prev] if (qgx > 0 and tc == 0) else [])
        qp_qT[qgx][tc] = transpose_tile(qTg, QG, 128 * tc, rv, tuple(wl))
    def emit_qp(qgx, tc, prev_ot_done):
        emit_qp_proj(qgx, tc, prev_ot_done)
        emit_qp_rope(qgx, tc)
    for _tc in range(TCQ):
        emit_qp(0, _tc, None)
    ev_ln4_prevqg = None
    for qg in range(NQG):
        S0, S1, P0, P1 = ROLE[qg]
        scp = (S0, S1)
        pvsets = [(P0[:, 0:512], P0[:, 512:1024]), (P1[:, 0:512], P1[:, 512:1024])]
        ev_qT_all = qp_qT[qg][TCQ - 1]
        if "qt" in dbg and qg == 0:
            op("gpsimd", lambda e: e.dma_start(out=dbg["qt"][:], in_=qTg[:]), waits=(ev_qT_all,), inc=sX, inc_by=16)

        ev_evac_done = [None, None]
        pend_pv = None
        pend_evac = [None]
        def make_evac(c, qg):
            pvA, pvB = pvsets[c % 2]
            def emit_evac():
                ev_pv_done = pend_evac_ev[0]
                wl0 = (ev_pv_done,) + ((ev_oproj_last_prev,) if (qg > 0 and c == 0) else ())
                if qg > 0 and c < 2:
                    wl0 = wl0 + (ev_out_store_prevqg[3],)
                if c == NC - 1:
                    op("scalar", (lambda e: e.activation(lnbuf, pvA[64:65, :], AF.Ln)), waits=wl0)
                    op("scalar", (lambda e: e.activation(recl[0:1, 0:QG], lnbuf, AF.Exp, scale=-1.0)))
                    op("scalar", (lambda e: e.activation(lnbuf, pvB[64:65, :], AF.Ln)))
                    op("scalar", (lambda e: e.activation(recl[0:1, QG:2 * QG], lnbuf, AF.Exp, scale=-1.0)),
                       inc=sS)
                    ev_recl = ev(sS)
                else:
                    def _rcA(e):
                        with nc.allow_low_precision(reason="softmax 1/l used as bf16 matmul operand"):
                            return e.reciprocal(recl[0:1, 0:QG], pvA[64:65, :])
                    op("vector", _rcA, waits=wl0)
                    def _rcB(e):
                        with nc.allow_low_precision(reason="softmax 1/l used as bf16 matmul operand"):
                            return e.reciprocal(recl[0:1, QG:2 * QG], pvB[64:65, :])
                    op("vector", _rcB, inc=sV)
                    ev_recl = ev(sV)
                op("vector", (lambda e: e.tensor_copy(rsc[0:64, :], pvB[0:64, :])), waits=wl0, inc=sV)
                ev_stag = ev(sV)
                op("tensor", (lambda e: e.matmul(pvB[0:64, :], onesb[0:1, 0:64], recl[0:1, 0:QG],
                                                 start=True, stop=True)), waits=(ev_recl, ev_stag))
                op("tensor", (lambda e: e.matmul(pvB[64:128, :], onesb[0:1, 0:64], recl[0:1, QG:2 * QG],
                                                 start=True, stop=True, tile_position=(0, 64))))
                op("tensor", (lambda e: e.matmul(pvA[64:128, :], id64b[:, :], rsc[0:64, :],
                                                 start=True, stop=True, tile_position=(0, 64))),
                   waits=((ev_id64,) if (qg == 0 and c == 0) else ()), inc=sT)
                ev_mv = ev(sT)
                op("vector", (lambda e: e.tensor_copy(repsb[:, :], pvB[:, :])), waits=(ev_mv,))
                odstA = OTg[0:64, QG * c: QG * c + QG]
                op("vector", (lambda e, o=odstA: e.tensor_tensor(o, pvA[0:64, :], repsb[0:64, :], ALU.mult)))
                odstB = OTg[64:128, QG * c: QG * c + QG]
                op("vector", (lambda e, o=odstB: e.tensor_tensor(o, pvA[64:128, :], repsb[64:128, :], ALU.mult)),
                   inc=sV)
                ev_evac_done[c % 2] = ev(sV)
            return emit_evac
        pend_evac_ev = [None]
        for c in range(NC):
            pvA, pvB = pvsets[c % 2]
            ev_sc = [None] * NT
            for kt in range(NT):
                psc = scp[0] if (kt % 2 == 0) else scp[1]
                w = []
                if kt == 0 and c == 0:
                    w.append(ev_qT_all)
                    if qg == 0:
                        w.append(ev_ph1_kT)
                if kt == 1 and c == 0 and qg > 0:
                    w.append(ev_ln4_prevqg[2])
                if ev_exp.get((qg, c, kt - 2)) is not None:
                    w.append(ev_exp[(qg, c, kt - 2)])
                if kt < 2 and c > 0:
                    w.append(ev_exp[(qg, c - 1, NT - 2 + kt)])
                lA = U1[0:64, T * c + 128 * kt: T * c + 128 * kt + 128]
                rA = qTg[0:64, QG * c: QG * c + QG]
                lB = U1[64:128, T * c + 128 * kt: T * c + 128 * kt + 128]
                rB = qTg[64:128, QG * c: QG * c + QG]
                op("tensor", (lambda e, o=psc[:, 0:512], l=lA, r=rA:
                              e.matmul(o, l, r, start=True, stop=True)), waits=tuple(w))
                op("tensor", (lambda e, o=psc[:, 512:1024], l=lB, r=rB:
                              e.matmul(o, l, r, start=True, stop=True, tile_position=(64, 0))), inc=sT)
                ev_sc[kt] = ev(sT)
                if kt % 2 == 1:
                    scbig = psA if qg % 2 == 0 else psB
                    we = [ev_sc[kt]]
                    if qg > 0 and c == 0 and kt == 1:
                        we.append(ev_out_store_prevqg[1])
                    op("scalar", (lambda e, o=PT[:, 0:2048], i=scbig[:, 0:2048]:
                                  e.activation(o, i, AF.Exp, scale=0.125)), waits=tuple(we), inc=sS)
                    ev_exp[(qg, c, kt)] = ev(sS)
                    ev_exp[(qg, c, kt - 1)] = ev_exp[(qg, c, kt)]
                if pend_pv is not None:
                    pend_pv(); pend_pv = None
                if kt == 6 and pend_evac[0] is not None:
                    pend_evac[0](); pend_evac[0] = None
                def make_pv(c=c, kt=kt, qg=qg, pvA=pvA, pvB=pvB):
                    def emit_pv():
                        w = [ev_exp[(qg, c, kt)]]
                        if kt == 0:
                            if qg == 0 and c == 0:
                                w.append(ev_ph1_v)
                            if ev_evac_done[c % 2] is not None:
                                w.append(ev_evac_done[c % 2])
                            if qg > 0 and c < 2:
                                w.append(ev_ln4_prevqg[1 if c == 0 else 3])
                        lA65 = U2[:, 1040 * kt + 65 * (2 * c): 1040 * kt + 65 * (2 * c) + 65]
                        lB65 = U2[:, 1040 * kt + 65 * (2 * c + 1): 1040 * kt + 65 * (2 * c + 1) + 65]
                        ptv2 = PT[:, 1024 * ((kt + qg) % 2):1024 * ((kt + qg) % 2) + 1024]
                        op("tensor", (lambda e, o=pvA[0:65, :], l=lA65, r=ptv2[:, 0:512], st=(kt == 0), sp=(kt == NT - 1):
                                      e.matmul(o, l, r, start=st, stop=sp)), waits=tuple(w))
                        op("tensor", (lambda e, o=pvB[0:65, :], l=lB65, r=ptv2[:, 512:1024], st=(kt == 0), sp=(kt == NT - 1):
                                      e.matmul(o, l, r, start=st, stop=sp)),
                           inc=(sT if kt == NT - 1 else None))
                    return emit_pv
                pend_pv = make_pv()
            pend_pv(); pend_pv = None
            pend_evac_ev[0] = ev(sT)
            pend_evac[0] = make_evac(c, qg)
        ev_evac6 = ev_evac_done[(NC - 2) % 2]
        ev_scores_done_prev = ev_exp[(qg, NC - 1, NT - 1)]
        ev_OT_done = None
        def flush_evac7():
            nonlocal ev_OT_done
            pend_evac[0](); pend_evac[0] = None
            ev_OT_done = ev_evac_done[(NC - 1) % 2]
            if "ot" in dbg and qg == 0:
                op("gpsimd", lambda e: e.dma_start(out=dbg["ot"][:], in_=OTg[:]), waits=(ev_OT_done,), inc=sX, inc_by=16)

        ev_ln2_apply = [None] * TCQ
        ev_ln4_apply = [None] * TCQ
        st_state = {}; st2_state = {}; ev_xd = {}
        def xload(tc, wl, qg=qg):
            g = qg * TCQ + tc
            xv = xs[:, 1024 * (tc % 2):1024 * (tc % 2) + 1024]
            op("sync", (lambda e, o=xv, i=x_h.rearrange("(g p) c -> g p c", p=128)[g]:
                        e.dma_start(out=o, in_=i)), waits=tuple(wl), inc=sD, inc_by=16)
            ev_xd[tc] = ev(sD)
            return xv
        def tail_A(tc, mid=None, qg=qg):
            if tc < 2:
                xv = xload(tc, [ev_x1_last_prev] if qg > 0 else [ev_h[NT - 2 + tc]])
            else:
                xv = xs[:, 1024 * (tc % 2):1024 * (tc % 2) + 1024]
            psO = (S0, S1, P0, S0)[tc]
            tmp4 = (hs[:, 0:1024], PT[:, 0:1024], rop[:, 0:1024], OTg[:, 0:1024])[tc]
            for ng in range(2):
                pss = psO[:, 512 * ng:512 * ng + 512]
                for kc in range(NC):
                    if ng == 0 and kc == NC - 1 and mid is not None:
                        mid()
                    lhsT = OTg[:, QG * kc + 128 * tc: QG * kc + 128 * tc + 128]
                    rhs = W1[:, 1024 * kc + 512 * ng: 1024 * kc + 512 * ng + 512]
                    w = []
                    if kc == 0 and ng == 0:
                        w = [ev_evac6, ev_wo]
                        if tc < 2:
                            w.append(ev_scores_done_prev)
                        if tc == 3:
                            w.append(ev_ln4_apply[0])
                    if kc == NC - 1:
                        w.append(ev_OT_done)
                    op("tensor", (lambda e, o=pss, l=lhsT, r=rhs, st=(kc == 0), sp=(kc == NC - 1):
                                  e.matmul(o, l, r, start=st, stop=sp)),
                       waits=tuple(w), inc=(sT if (kc == NC - 1 and ng == 1) else None))
            ev_oproj = ev(sT)
            x1v = x1g[:, 1024 * tc:1024 * tc + 1024]
            st_state[tc] = (ev_oproj, x1v, xv, psO, tmp4)
        def tail_B1(tc, rope_job=None, qg=qg):
            nonlocal ev_ffn_prev, ev_oproj_last_prev
            ev_oproj, x1v, xv, psO, tmp4 = st_state[tc]
            if tc == TCQ - 1:
                ev_oproj_last_prev = ev_oproj
            evr2, negm2, rstd2, nmr2 = ln_stats(psO[:, 0:1024], (ev_oproj,))
            op("scalar", (lambda e, s=psO, b=nmr2, rs=rstd2, t=tmp4:
                          e.activation(t, s[:, 0:1024], AF.Identity, bias=b, scale=rs)),
               waits=(evr2,), inc=sS)
            ev_ln2_apply[tc] = ev(sS)
            wl2 = [ev_ln2_apply[tc], ev_xd[tc]]
            if qg > 0:
                wl2.append(ev_out_store_prevqg[tc])
            op("vector", (lambda e, o=x1v, i=xv, t=tmp4: e.tensor_tensor(o, i, t, ALU.add)),
               waits=tuple(wl2), inc=sV)
            ev_x1_done[tc] = ev(sV)
            if tc < 2:
                xload(tc + 2, [ev_x1_done[tc]])
            evr3, negm3, rstd3, nmr3 = ln_stats(x1v, (ev_x1_done[tc],))
            wl = [evr3] + ([ev_ffn_prev] if ev_ffn_prev is not None else [])
            op("scalar", (lambda e, i=x1v, b=nmr3, rs=rstd3:
                          e.activation(h2s[:, 0:512], i[:, 0:512], AF.Identity, bias=b, scale=rs)),
               waits=tuple(wl), inc=sS)
            ev_h2a = ev(sS)
            op("scalar", (lambda e, i=x1v, b=nmr3, rs=rstd3:
                          e.activation(h2s[:, 512:1024], i[:, 512:1024], AF.Identity, bias=b, scale=rs)),
               inc=sS)
            ev_h2b = ev(sS)
            ev_h2Ta = transpose_tile_ap(h2TgT[:, 0:512], 128, 0, h2s[:, 0:512], (ev_h2a,))
            ev_h2T = transpose_tile_ap(h2TgT[:, 512:1024], 128, 0, h2s[:, 512:1024], (ev_h2b,))
            if rope_job is not None:
                rope_job()
            psY = psO
            for ng in range(2):
                pss = psY[:, 512 * ng:512 * ng + 512]
                for kc in range(NC):
                    lhsT = h2TgT[:, 128 * kc:128 * kc + 128]
                    rhs = Msb[:, 1024 * kc + 512 * ng: 1024 * kc + 512 * ng + 512]
                    w = []
                    if kc == 0 and ng == 0:
                        w = [ev_h2Ta, ev_m_done, ev_bias]
                    if kc == 4 and ng == 0:
                        w = [ev_h2T]
                    op("tensor", (lambda e, o=pss, l=lhsT, r=rhs, st=(kc == 0):
                                  e.matmul(o, l, r, start=st, stop=False)),
                       waits=tuple(w))
                op("tensor", (lambda e, o=pss, r=biasb[0:1, 512 * ng:512 * ng + 512]:
                              e.matmul(o, onesb[0:1, 0:128], r, start=False, stop=True)),
                   inc=(sT if ng == 1 else None))
            ev_ffn = ev(sT)
            ev_ffn_prev = ev_ffn
            st2_state[tc] = ev_ffn
        def tail_B2a(tc, rope_job=None, qg=qg):
            ev_oproj, x1v, xv, psY, tmp4 = st_state[tc]
            ev_ffn = st2_state[tc]
            evr4, negm4, rstd4, nmr4 = ln_stats(psY[:, 0:1024], (ev_ffn,))
            if rope_job is not None:
                rope_job()
            wl4 = [evr4]
            if tc == 2 and qg + 1 < NQG:
                wl4.append(qp_qT[qg + 1][2])
            op("scalar", (lambda e, s=psY, b=nmr4, rs=rstd4, t=tmp4:
                          e.activation(t, s[:, 0:1024], AF.Identity, bias=b, scale=rs)),
               waits=tuple(wl4), inc=sS)
            ev_ln4_apply[tc] = ev(sS)
        def tail_B2b(tc, qg=qg):
            nonlocal ev_out_store_prev
            g = qg * TCQ + tc
            ev_oproj, x1v, xv, psY, tmp4 = st_state[tc]
            op("gpsimd", (lambda e, i=x1v, t=tmp4: e.tensor_tensor(i, i, t, ALU.add)),
               waits=(ev_ln4_apply[tc],), inc=sP)
            ev_o = ev(sP)
            op("gpsimd", (lambda e, i=x1v, o=out_h.rearrange("(g p) c -> g p c", p=128)[g]:
                        e.dma_start(out=o, in_=i)), waits=(ev_o,), inc=sX, inc_by=16)
            ev_out_store[tc] = ev(sX)
            ev_out_store_prev = ev(sX)
        ev_out_store = ev_out_store if qg > 0 else [None] * TCQ
        nq = qg + 1 if qg + 1 < NQG else None
        tail_A(0, mid=flush_evac7); tail_A(1); tail_A(2)
        if nq is not None: emit_qp_proj(nq, 0, ev_OT_done)
        tail_B1(0, (lambda: emit_qp_rope(nq, 0)) if nq is not None else None)
        if nq is not None: emit_qp_proj(nq, 1, ev_OT_done)
        tail_B1(1)
        tail_B2a(0, (lambda: emit_qp_rope(nq, 1)) if nq is not None else None)
        if nq is not None: emit_qp_proj(nq, 2, ev_OT_done)
        tail_A(3)
        tail_B1(2)
        tail_B2a(1, (lambda: emit_qp_rope(nq, 2)) if nq is not None else None)
        tail_B2b(0)
        if nq is not None: emit_qp_proj(nq, 3, ev_OT_done)
        tail_B1(3)
        tail_B2a(2, (lambda: emit_qp_rope(nq, 3)) if nq is not None else None)
        tail_B2b(1)
        tail_B2a(3)
        tail_B2b(2)
        tail_B2b(3)
        ev_ln4_prevqg = ev_ln4_apply
        if "x1" in dbg and qg == 0:
            op("sync", lambda e: e.dma_start(out=dbg["x1"][:], in_=x1g[:]), waits=(ev_x1_done[TCQ - 1],), inc=sX, inc_by=16)
        ev_x1_last_prev = ev_x1_done[TCQ - 1]
        ev_out_store_last_prev = ev_out_store[TCQ - 1]
        ev_out_store_prevqg = list(ev_out_store)

    ops["sync"].append(lambda e: e.wait_ge(sX.h, sX.n))

    blk = E(nc.Block())
    @blk.sync
    def _(e):
        for f in ops["sync"]: f(e)
    @blk.gpsimd
    def _(e):
        for f in ops["gpsimd"]: f(e)
    @blk.tensor
    def _(e):
        for f in ops["tensor"]: f(e)
    @blk.vector
    def _(e):
        for f in ops["vector"]: f(e)
    @blk.scalar
    def _(e):
        for f in ops["scalar"]: f(e)
    es.close()
    return nc, dbg

def make_inmaps(ins):
    Cp, Sp, cosTt, sinTt, ones128, ident64 = host_constants()
    f = perm_f_of_pos()
    base = dict(
        wqpt=np.ascontiguousarray(ins["L0_attn_Wq"][f].T),
        wkpt=np.ascontiguousarray(ins["L0_attn_Wk"][f].T),
        wvt=np.ascontiguousarray(ins["L0_attn_Wv"].T),
        wot=np.ascontiguousarray(ins["L0_attn_Wo"].T),
        cp=Cp, sp=Sp,
        wrt=np.ascontiguousarray(ins["L0_ffn_w_freq_real"].T),
        wit=np.ascontiguousarray(ins["L0_ffn_w_freq_imag"].T),
        biasrow=np.ascontiguousarray(ins["L0_ffn_bias"].reshape(1, D)),
        cost=cosTt, sint=sinTt, ones128=ones128, ident64=ident64)
    x = ins["L0_attn_x"]
    return [dict(base, x=np.ascontiguousarray(x[b])) for b in range(x.shape[0])]

_CACHE = {}

def kernel(L0_attn_x, L0_attn_Wq, L0_attn_Wk, L0_attn_Wv, L0_attn_Wo,
           L0_ffn_w_freq_real, L0_ffn_w_freq_imag, L0_ffn_bias):
    ins = dict(L0_attn_x=np.asarray(L0_attn_x), L0_attn_Wq=np.asarray(L0_attn_Wq),
               L0_attn_Wk=np.asarray(L0_attn_Wk), L0_attn_Wv=np.asarray(L0_attn_Wv),
               L0_attn_Wo=np.asarray(L0_attn_Wo),
               L0_ffn_w_freq_real=np.asarray(L0_ffn_w_freq_real),
               L0_ffn_w_freq_imag=np.asarray(L0_ffn_w_freq_imag),
               L0_ffn_bias=np.asarray(L0_ffn_bias))
    if "nc" not in _CACHE:
        _CACHE["nc"] = build()[0]
    nc = _CACHE["nc"]
    in_maps = make_inmaps(ins)
    res = run_bass_kernel_spmd(nc, in_maps, core_ids=list(range(8)))
    out = np.stack([np.asarray(r["out"]) for r in res.results], axis=0)
    return out.astype(np.float32)


# version 21
# speedup vs baseline: 1.4738x; 1.4738x over previous
import numpy as np
import ml_dtypes
import concourse.bass as bass
import concourse.mybir as mybir
from concourse.bass_utils import run_bass_kernel_spmd
from contextlib import ExitStack

F32, BF16 = mybir.dt.float32, mybir.dt.bfloat16
AF, ALU = mybir.ActivationFunctionType, mybir.AluOpType
bfd = ml_dtypes.bfloat16

T, D, NH, DH = 2048, 1024, 16, 64
NT = T // 128
NC = D // 128
QG = 512
NQG = T // QG
TCQ = QG // 128
EPS = 1e-5

def host_constants():
    j = np.arange(D)
    ang = 2.0 * np.pi * np.outer(j, j) / D
    Cp = (np.cos(ang) / 32.0).astype(np.float32)
    Sp = (np.sin(ang) / 32.0).astype(np.float32)
    inv = 1.0 / 10000.0 ** (np.arange(0, DH, 2, dtype=np.float32) / DH)
    freqs = np.arange(T, dtype=np.float32)[:, None] * inv[None, :]
    cosT = np.cos(freqs); sinT = np.sin(freqs)
    cosTt = np.ascontiguousarray(cosT.reshape(NT, 128, 32).transpose(1, 0, 2).reshape(128, NT * 32))
    sinTt = np.ascontiguousarray(sinT.reshape(NT, 128, 32).transpose(1, 0, 2).reshape(128, NT * 32))
    ones128 = np.ones((1, 128), dtype=bfd)
    ident64 = np.eye(64, dtype=bfd)
    return Cp, Sp, cosTt, sinTt, ones128, ident64

def perm_f_of_pos():
    p = np.arange(D); head = p // DH; r = p % DH
    return np.where(r < 32, head * DH + 2 * r, head * DH + 2 * (r - 32) + 1)

class Sem:
    __slots__ = ("name", "n", "h")
    def __init__(s, name): s.name, s.n, s.h = name, 0, None

def build(debug=()):
    nc = bass.Bass("TRN2", target_bir_lowering=False)
    H = lambda n, shp, dt=F32, kind="ExternalInput": nc.dram_tensor(n, shp, dt, kind=kind).ap()
    x_h   = H("x", [T, D])
    wq_h  = H("wqpt", [D, D]); wk_h = H("wkpt", [D, D])
    wv_h  = H("wvt", [D, D]);  wo_h = H("wot", [D, D])
    cp_h  = H("cp", [D, D]);   sp_h = H("sp", [D, D])
    wrt_h = H("wrt", [D, D]);  wit_h = H("wit", [D, D])
    bias_h = H("biasrow", [1, D])
    cosT_h = H("cost", [128, NT * 32]); sinT_h = H("sint", [128, NT * 32])
    ones_h = H("ones128", [1, 128], BF16)
    id64_h = H("ident64", [64, 64], BF16)
    out_h = H("out", [T, D], F32, kind="ExternalOutput")
    dbg = {k: H("dbg_" + k, shp, F32, kind="ExternalOutput") for k, shp in {
        "m":   [128, NC * 1024],
        "ht":  [128, NC * T],
        "kt":  [128, NC * T],
        "v65": [128, NT * NH * 65],
        "qt":  [128, NC * QG],
        "ot":  [128, NC * QG],
        "x1":  [128, TCQ * 1024],
    }.items() if k in debug}

    es = ExitStack(); E = es.enter_context
    U0 = E(nc.sbuf_tensor("U0", [128, 16384], BF16))
    U1 = E(nc.sbuf_tensor("U1", [128, 16384], BF16))
    U2 = E(nc.sbuf_tensor("U2", [128, 16640], BF16))
    Msb = E(nc.sbuf_tensor("Msb", [128, 8192], BF16))
    W0 = E(nc.sbuf_tensor("W0", [128, 8192], BF16))
    W1 = E(nc.sbuf_tensor("W1", [128, 8192], BF16))
    xs  = E(nc.sbuf_tensor("xs", [128, 2 * 1024], F32))
    hs  = E(nc.sbuf_tensor("hs", [128, 2 * 1024], BF16))
    rop = E(nc.sbuf_tensor("rop", [128, 2 * 1024], BF16))
    rsc = E(nc.sbuf_tensor("rsc", [128, 512], BF16))
    PT  = E(nc.sbuf_tensor("PT", [128, 2 * 1024], BF16))
    cosb = E(nc.sbuf_tensor("cosb", [128, NT * 32], BF16))
    sinb = E(nc.sbuf_tensor("sinb", [128, NT * 32], BF16))
    onesb = E(nc.sbuf_tensor("onesb", [1, 128], BF16))
    id64b = E(nc.sbuf_tensor("id64b", [64, 64], BF16))
    biasb = E(nc.sbuf_tensor("biasb", [1, D], BF16))
    qTg = E(nc.sbuf_tensor("qTg", [128, NC * QG], BF16))
    OTg = E(nc.sbuf_tensor("OTg", [128, NC * QG], BF16))
    recl = E(nc.sbuf_tensor("recl", [1, 2 * QG], BF16))
    x1g = E(nc.sbuf_tensor("x1g", [128, TCQ * 1024], F32))
    lnbuf = x1g[0:1, 3584:4096]
    tmpg = hs[:, 0:1024]
    h2s = hs[:, 1024:2048]

    bst = E(nc.sbuf_tensor("bst", [128, 4 * 12], F32))
    agA = E(nc.sbuf_tensor("agA", [128, 4 * 2], F32))
    agB = E(nc.sbuf_tensor("agB", [128, 4 * 2], F32))
    stt = E(nc.sbuf_tensor("stt", [128, 4 * 10], F32))
    epsb = E(nc.sbuf_tensor("epsb", [128, 1], F32))
    halfb = E(nc.sbuf_tensor("halfb", [128, 1], F32))
    nhalfb = E(nc.sbuf_tensor("nhalfb", [128, 1], F32))
    repsb = E(nc.sbuf_tensor("repsb", [128, QG], BF16))
    h2TgT = E(nc.sbuf_tensor("h2TgT", [128, 1024], BF16))

    psA = E(nc.psum_tensor("psA", [128, 2048], F32))
    psB = E(nc.psum_tensor("psB", [128, 2048], F32))
    ps0 = psA[:, 0:1024]
    ps1 = psA[:, 1024:2048]
    ps2 = psB[:, 0:1024]
    ps3 = psB[:, 1024:2048]
    ROLE = {0: (ps0, ps1, ps2, ps3)}
    for _q in range(1, NQG):
        _s0, _s1, _p0, _p1 = ROLE[_q - 1]
        ROLE[_q] = (_p1, _p0, _s1, _s0)

    sD = Sem("sD"); sG = Sem("sG"); sT = Sem("sT"); sV = Sem("sV"); sS = Sem("sS")
    sP = Sem("sP"); sX = Sem("sX"); sH = Sem("sH")
    for s in (sD, sG, sT, sV, sS, sP, sX, sH):
        s.h = E(nc.semaphore(s.name))

    ops = {k: [] for k in ("sync", "gpsimd", "tensor", "vector", "scalar")}
    def op(eng, emit, waits=(), inc=None, inc_by=1):
        wl = tuple(w for w in waits if w is not None)
        def run(e, emit=emit, wl=wl, inc=inc, inc_by=inc_by):
            for (s, v) in wl:
                e.wait_ge(s.h, v)
            ins = emit(e)
            if inc is not None:
                ins.then_inc(inc.h, inc_by)
        ops[eng].append(run)
        if inc is not None:
            inc.n += inc_by
    def ev(s): return (s, s.n)

    op("gpsimd", lambda e: e.dma_start(out=cosb[:], in_=cosT_h[:]), inc=sG, inc_by=16)
    op("gpsimd", lambda e: e.dma_start(out=sinb[:], in_=sinT_h[:]), inc=sG, inc_by=16)
    op("sync", lambda e: e.dma_start(out=onesb[:], in_=ones_h[:]), inc=sD, inc_by=16)
    op("sync", lambda e: e.dma_start(out=id64b[:], in_=id64_h[:]), inc=sD, inc_by=16)
    ev_id64 = ev(sD)
    def wload(dstview, src):
        i3 = src.rearrange("(r p) c -> p r c", p=128)
        o3 = dstview.rearrange("p (r c) -> p r c", c=1024)
        return lambda e, i3=i3, o3=o3: e.dma_start(out=o3, in_=i3)
    op("gpsimd", wload(U0[:, 0:8192], cp_h), inc=sG, inc_by=16);     ev_cp = ev(sG)
    op("gpsimd", wload(U0[:, 8192:16384], sp_h), inc=sG, inc_by=16); ev_sp = ev(sG)
    op("gpsimd", wload(U1[:, 0:8192], wrt_h), inc=sG, inc_by=16);    ev_wr = ev(sG)
    op("gpsimd", wload(U1[:, 8192:16384], wit_h), inc=sG, inc_by=16); ev_wi = ev(sG)
    op("gpsimd", wload(W0[:, 0:8192], wk_h), inc=sG, inc_by=16);     ev_wk = ev(sG)
    op("gpsimd", wload(W1[:, 0:8192], wv_h), inc=sG, inc_by=16);     ev_wv = ev(sG)
    op("gpsimd", lambda e: e.dma_start(out=biasb[:], in_=bias_h[:]), inc=sG, inc_by=16); ev_bias = ev(sG)

    Cp_v = U0[:, 0:8192]; Sp_v = U0[:, 8192:16384]
    WrT_v = U1[:, 0:8192]; WiT_v = U1[:, 8192:16384]
    Br_v = U2[:, 0:8192]; Bi_v = U2[:, 8192:16384]

    def phase0_product(dst_view, pairs, first_waits):
        evacs = []
        idx = 0
        for mc in range(8):
            for ng in range(2):
                ps = ps0 if (idx % 2 == 0) else ps1
                pss = ps[:, 0:512]
                n0 = ng * 512
                nmm = len(pairs) * 8
                i = 0
                for (lv, rv) in pairs:
                    for kc in range(8):
                        lhsT = lv[:, 1024 * kc + 128 * mc: 1024 * kc + 128 * mc + 128]
                        rhs = rv[:, 1024 * kc + n0: 1024 * kc + n0 + 512]
                        w = []
                        if i == 0:
                            if idx == 0:
                                w = list(first_waits)
                            if idx >= 2:
                                w.append(evacs[idx - 2])
                        last = (i == nmm - 1)
                        op("tensor", (lambda e, o=pss, l=lhsT, r=rhs, st=(i == 0), sp=last:
                                      e.matmul(o, l, r, start=st, stop=sp)),
                           waits=tuple(w), inc=(sT if last else None))
                        i += 1
                evT = ev(sT)
                dstv = dst_view[:, 1024 * mc + n0: 1024 * mc + n0 + 512]
                op("vector", (lambda e, o=dstv, i2=pss: e.tensor_copy(o, i2)), waits=(evT,), inc=sV)
                evacs.append(ev(sV))
                idx += 1
        return evacs

    phase0_product(Bi_v, [(WrT_v, Sp_v), (WiT_v, Cp_v)], (ev_cp, ev_sp, ev_wr, ev_wi))
    op("vector", lambda e: e.tensor_scalar(WiT_v, WiT_v, -1.0, None, ALU.mult), inc=sV)
    ev_win = ev(sV)
    phase0_product(Br_v, [(WrT_v, Cp_v), (WiT_v, Sp_v)], (ev_win,))
    ev_br_done = ev(sV)
    mevacs = phase0_product(Msb[:, 0:8192], [(Cp_v, Br_v), (Sp_v, Bi_v)], (ev_br_done,))
    ev_mT_last = ev(sT)
    ev_m_done = ev(sV)
    if "m" in dbg:
        op("gpsimd", lambda e: e.dma_start(out=dbg["m"][:], in_=Msb[:]), waits=(ev_m_done,), inc=sX, inc_by=16)

    op("vector", lambda e: e.memset(epsb[:], EPS))
    op("vector", lambda e: e.memset(halfb[:], 0.5))
    op("vector", lambda e: e.memset(nhalfb[:], -0.5), inc=sV)
    last_comb = [None, None, None, None]
    ln_ctr = [0]
    def ln_stats2(srcA, srcB, waits):
        sl = ln_ctr[0] % 4; ln_ctr[0] += 1
        b0 = bst[:, 12 * sl:12 * sl + 6]; b1 = bst[:, 12 * sl + 6:12 * sl + 12]
        aA = agA[:, 2 * sl:2 * sl + 2]; aB = agB[:, 2 * sl:2 * sl + 2]
        s_ = stt[:, 10 * sl:10 * sl + 10]
        w = list(waits) + ([last_comb[sl]] if last_comb[sl] else [])
        op("vector", lambda e: e.bn_stats(b0, srcA), waits=tuple(w))
        op("vector", lambda e: e.bn_stats(b1, srcB))
        op("vector", lambda e: e.bn_aggr(aA, b0))
        op("vector", lambda e: e.bn_aggr(aB, b1), inc=sV)
        evagg = ev(sV)
        op("gpsimd", lambda e: e.tensor_tensor(s_[:, 0:1], aA[:, 0:1], aB[:, 0:1], ALU.add), waits=(evagg,))
        op("gpsimd", lambda e: e.tensor_tensor(s_[:, 1:2], aA[:, 0:1], aB[:, 0:1], ALU.subtract))
        op("gpsimd", lambda e: e.tensor_tensor(s_[:, 2:3], aA[:, 1:2], aB[:, 1:2], ALU.add))
        op("gpsimd", lambda e: e.tensor_tensor(s_[:, 3:4], s_[:, 1:2], s_[:, 1:2], ALU.mult))
        op("gpsimd", lambda e: e.tensor_tensor(s_[:, 3:4], s_[:, 3:4], halfb[:], ALU.mult))
        op("gpsimd", lambda e: e.tensor_tensor(s_[:, 4:5], s_[:, 3:4], s_[:, 2:3], ALU.add))
        op("gpsimd", lambda e: e.tensor_tensor(s_[:, 6:7], s_[:, 0:1], nhalfb[:], ALU.mult), inc=sP)
        evcomb = ev(sP)
        last_comb[sl] = evcomb
        op("scalar", lambda e: e.activation(s_[:, 5:6], s_[:, 4:5], AF.Ln, bias=epsb[:], scale=0.5),
           waits=(evcomb,))
        op("scalar", lambda e: e.activation(s_[:, 7:8], s_[:, 5:6], AF.Exp, scale=-0.5), inc=sS)
        evsq = ev(sS)
        op("gpsimd", lambda e: e.tensor_tensor(s_[:, 8:9], s_[:, 6:7], s_[:, 7:8], ALU.mult), waits=(evsq,), inc=sP)
        return ev(sP), s_[:, 6:7], s_[:, 7:8], s_[:, 8:9]
    def ln_stats(src_view, waits):
        return ln_stats2(src_view[:, 0:512], src_view[:, 512:1024], waits)

    def rope_tile(ps_view, out_tile, g, waits):
        ct = cosb[:, 32 * g:32 * g + 32].rearrange("p (a b) -> p a b", a=1).to_broadcast((128, 16, 32))
        st = sinb[:, 32 * g:32 * g + 32].rearrange("p (a b) -> p a b", a=1).to_broadcast((128, 16, 32))
        p4 = ps_view.rearrange("p (h two j) -> p h two j", two=2, j=32)
        o4 = out_tile.rearrange("p (h two j) -> p h two j", two=2, j=32)
        e_in, o_in = p4[:, :, 0:1, :], p4[:, :, 1:2, :]
        e_out, o_out = o4[:, :, 0:1, :], o4[:, :, 1:2, :]
        sc1 = rsc[:].rearrange("p (h j) -> p h j", j=32)
        op("vector", lambda e: e.tensor_tensor(sc1, o_in, st, ALU.mult), waits=waits)
        op("vector", lambda e: e.tensor_tensor(e_out, e_in, ct, ALU.mult))
        op("vector", lambda e: e.tensor_tensor(e_out, e_out, sc1, ALU.subtract))
        op("vector", lambda e: e.tensor_tensor(sc1, o_in, ct, ALU.mult))
        op("vector", lambda e: e.tensor_tensor(o_out, e_in, st, ALU.mult))
        op("vector", lambda e: e.tensor_tensor(o_out, o_out, sc1, ALU.add), inc=sV)
        return ev(sV)

    def transpose_tile(dst, chunk_len, col_off, src_tile, waits):
        return transpose_tile_ap(dst[:], chunk_len, col_off, src_tile, waits)
    def transpose_tile_ap(dstap, chunk_len, col_off, src_ap, waits):
        outap = dstap.rearrange("p (c t) -> p c t", t=chunk_len)[:, :, col_off:col_off + 128]
        inap = src_ap.rearrange("p (c j) -> p c j", j=128)
        op("sync", (lambda e, o=outap, i=inap: e.dma_start(out=o, in_=i, transpose=True)),
           waits=waits, inc=sD, inc_by=16)
        return ev(sD)

    op("vector", lambda e: e.memset(U2[:], 1.0), inc=sV)
    ev_v65init = ev(sV)
    ev_hT = [None] * NT; ev_kT = [None] * NT; ev_vev = [None] * NT
    ev_h = [None] * NT; ev_rope1 = [None] * NT
    ev_x1load = [None] * NT
    def ph1_xload(g):
        xv = xs[:, 1024 * (g % 2):1024 * (g % 2) + 1024]
        wl = [(sH, g - 1)] if g >= 2 else []
        op("sync", (lambda e, o=xv, i=x_h.rearrange("(g p) c -> g p c", p=128)[g]:
                    e.dma_start(out=o, in_=i)), waits=tuple(wl), inc=sD, inc_by=16)
        ev_x1load[g] = ev(sD)
    def ph1_hoist(g):
        xv = xs[:, 1024 * (g % 2):1024 * (g % 2) + 1024]
        evr, negm, rstd, nmr = ln_stats(xv, (ev_x1load[g],))
        hv = hs[:, 1024 * (g % 2):1024 * (g % 2) + 1024]
        wl = [evr] + ([ev_hT[g - 2]] if g >= 2 else [])
        op("scalar", (lambda e, o=hv, i=xv, b=nmr, rs=rstd: e.activation(o, i, AF.Identity, bias=b, scale=rs)),
           waits=tuple(wl), inc=sH)
        op("scalar", lambda e: e.nop(), inc=sV)
        ev_h[g] = ev(sV)
        wl = [ev_h[g]] + ([ev_mT_last] if g == 0 else [])
        ev_hT[g] = transpose_tile(U0, T, 128 * g, hv, tuple(wl))
        if g + 4 < NT:
            ph1_xload(g + 4)
    for g in range(4):
        ph1_xload(g)
    ph1_hoist(0); ph1_hoist(1)
    for g in range(NT):
        if g + 2 < NT:
            ph1_hoist(g + 2)
        psK = ps0 if (g % 2 == 0) else ps2
        for ng in range(2):
            pss = psK[:, 512 * ng:512 * ng + 512]
            for kc in range(NC):
                lhsT = U0[:, T * kc + 128 * g: T * kc + 128 * g + 128]
                rhs = W0[:, 1024 * kc + 512 * ng: 1024 * kc + 512 * ng + 512]
                w = []
                if kc == 0 and ng == 0:
                    w = [ev_hT[g], ev_wk] + ([ev_rope1[g - 2]] if g >= 2 else [])
                op("tensor", (lambda e, o=pss, l=lhsT, r=rhs, st=(kc == 0), sp=(kc == NC - 1):
                              e.matmul(o, l, r, start=st, stop=sp)),
                   waits=tuple(w), inc=(sT if (kc == NC - 1 and ng == 1) else None))
        ev_kproj = ev(sT)
        if g == NT - 1:
            ev_kproj15 = ev_kproj
        rv = rop[:, 1024 * (g % 2):1024 * (g % 2) + 1024]
        wl = [ev_kproj] + ([ev_kT[g - 2]] if g >= 2 else [])
        ev_rope1[g] = rope_tile(psK[:, 0:1024], rv, g, tuple(wl))
        ev_kT[g] = transpose_tile(U1, T, 128 * g, rv, (ev_rope1[g],) + ((ev_mT_last,) if g == 0 else ()))
        psV = ps1 if (g % 2 == 0) else ps3
        for ng in range(2):
            pss = psV[:, 512 * ng:512 * ng + 512]
            for kc in range(NC):
                lhsT = U0[:, T * kc + 128 * g: T * kc + 128 * g + 128]
                rhs = W1[:, 1024 * kc + 512 * ng: 1024 * kc + 512 * ng + 512]
                w = []
                if kc == 0 and ng == 0:
                    w = [ev_hT[g], ev_wv, ev_v65init] + ([ev_vev[g - 2]] if g >= 2 else [])
                op("tensor", (lambda e, o=pss, l=lhsT, r=rhs, st=(kc == 0), sp=(kc == NC - 1):
                              e.matmul(o, l, r, start=st, stop=sp)),
                   waits=tuple(w), inc=(sT if (kc == NC - 1 and ng == 1) else None))
        ev_vproj = ev(sT)
        vdst = U2[:, 1040 * g:1040 * g + 1040].rearrange("p (h c) -> p h c", c=65)[:, :, 0:64]
        vsrc = psV[:, 0:1024].rearrange("p (h c) -> p h c", c=64)
        op("vector", (lambda e, o=vdst, i=vsrc: e.tensor_copy(o, i)), waits=(ev_vproj,), inc=sV)
        ev_vev[g] = ev(sV)
    ev_ph1_v = ev(sV)
    ev_ph1_kT = ev_kT[NT - 1]
    ev_ph1_T = ev(sT)
    if "ht" in dbg:
        op("gpsimd", lambda e: e.dma_start(out=dbg["ht"][:], in_=U0[:]), waits=(ev_hT[NT - 1],), inc=sX, inc_by=16)
    if "kt" in dbg:
        op("gpsimd", lambda e: e.dma_start(out=dbg["kt"][:], in_=U1[:]), waits=(ev_ph1_kT,), inc=sX, inc_by=16)
    if "v65" in dbg:
        op("gpsimd", lambda e: e.dma_start(out=dbg["v65"][:], in_=U2[:]), waits=(ev_ph1_v,), inc=sX, inc_by=16)

    op("gpsimd", wload(W0[:, 0:8192], wq_h), waits=(ev_kproj15,), inc=sG, inc_by=16)
    ev_wq = ev(sG)
    op("gpsimd", wload(W1[:, 0:8192], wo_h), waits=(ev_ph1_T,), inc=sG, inc_by=16)
    ev_wo = ev(sG)

    ev_exp = {}
    ev_scores_done_prev = None
    ev_oproj_last_prev = None
    ev_x1_done = [None] * TCQ
    ev_x1_last_prev = None
    ev_out_store_prev = None
    ev_ffn_prev = None
    qp_rope = {}; qp_qT = {}; qp_proj = {}
    def emit_qp_proj(qgx, tc, prev_ot_done):
        if qgx not in qp_rope:
            qp_rope[qgx] = [None] * TCQ; qp_qT[qgx] = [None] * TCQ; qp_proj[qgx] = [None] * TCQ
        g = qgx * TCQ + tc
        if qgx == 0:
            psf = (ps2, ps3, ps0, ps1)[tc]
        else:
            psf = ROLE[qgx - 1][3]
        for ng in range(2):
            pss = psf[:, 512 * ng:512 * ng + 512]
            for kc in range(NC):
                lhsT = U0[:, T * kc + 128 * g: T * kc + 128 * g + 128]
                rhs = W0[:, 1024 * kc + 512 * ng: 1024 * kc + 512 * ng + 512]
                w = []
                if kc == 0 and ng == 0:
                    w = [ev_wq]
                    if tc >= 2:
                        w.append(qp_rope[qgx][tc - 2])
                    if qgx == 0 and tc == 0:
                        w.append(ev_rope1[NT - 1])
                    if qgx == 0 and tc == 1:
                        w.append(ev_vev[NT - 1])
                    if qgx > 0:
                        w.append(prev_ot_done if tc == 0 else qp_rope[qgx][tc - 1])
                op("tensor", (lambda e, o=pss, l=lhsT, r=rhs, st=(kc == 0), sp=(kc == NC - 1):
                              e.matmul(o, l, r, start=st, stop=sp)),
                   waits=tuple(w), inc=(sT if (kc == NC - 1 and ng == 1) else None))
        qp_proj[qgx][tc] = ev(sT)
    def emit_qp_rope(qgx, tc):
        g = qgx * TCQ + tc
        psf = (ps2, ps3, ps0, ps1)[tc] if qgx == 0 else ROLE[qgx - 1][3]
        rv = rop[:, 1024 * (tc % 2):1024 * (tc % 2) + 1024]
        wl = [qp_proj[qgx][tc]] + ([qp_qT[qgx][tc - 2]] if tc >= 2 else [])
        if tc < 2:
            wl.append(ev_kT[NT - 2 + tc] if qgx == 0 else qp_qT[qgx - 1][TCQ - 2 + tc])
        qp_rope[qgx][tc] = rope_tile(psf[:, 0:1024], rv, g, tuple(wl))
        wl = [qp_rope[qgx][tc]] + ([ev_scores_done_prev] if (qgx > 0 and tc == 0) else [])
        qp_qT[qgx][tc] = transpose_tile(qTg, QG, 128 * tc, rv, tuple(wl))
    def emit_qp(qgx, tc, prev_ot_done):
        emit_qp_proj(qgx, tc, prev_ot_done)
        emit_qp_rope(qgx, tc)
    for _tc in range(TCQ):
        emit_qp(0, _tc, None)
    ev_ln4_prevqg = None
    for qg in range(NQG):
        S0, S1, P0, P1 = ROLE[qg]
        scp = (S0, S1)
        pvsets = [(P0[:, 0:512], P0[:, 512:1024]), (P1[:, 0:512], P1[:, 512:1024])]
        ev_qT_all = qp_qT[qg][TCQ - 1]
        if "qt" in dbg and qg == 0:
            op("gpsimd", lambda e: e.dma_start(out=dbg["qt"][:], in_=qTg[:]), waits=(ev_qT_all,), inc=sX, inc_by=16)

        ev_evac_done = [None, None]
        pend_pv = None
        pend_evac = [None]
        def make_evac(c, qg):
            pvA, pvB = pvsets[c % 2]
            def emit_evac():
                ev_pv_done = pend_evac_ev[0]
                wl0 = (ev_pv_done,) + ((ev_oproj_last_prev,) if (qg > 0 and c == 0) else ())
                if qg > 0 and c < 2:
                    wl0 = wl0 + (ev_out_store_prevqg[3],)
                if c == NC - 1:
                    op("scalar", (lambda e: e.activation(lnbuf, pvA[64:65, :], AF.Ln)), waits=wl0)
                    op("scalar", (lambda e: e.activation(recl[0:1, 0:QG], lnbuf, AF.Exp, scale=-1.0)))
                    op("scalar", (lambda e: e.activation(lnbuf, pvB[64:65, :], AF.Ln)))
                    op("scalar", (lambda e: e.activation(recl[0:1, QG:2 * QG], lnbuf, AF.Exp, scale=-1.0)),
                       inc=sS)
                    ev_recl = ev(sS)
                else:
                    def _rcA(e):
                        with nc.allow_low_precision(reason="softmax 1/l used as bf16 matmul operand"):
                            return e.reciprocal(recl[0:1, 0:QG], pvA[64:65, :])
                    op("vector", _rcA, waits=wl0)
                    def _rcB(e):
                        with nc.allow_low_precision(reason="softmax 1/l used as bf16 matmul operand"):
                            return e.reciprocal(recl[0:1, QG:2 * QG], pvB[64:65, :])
                    op("vector", _rcB, inc=sV)
                    ev_recl = ev(sV)
                op("vector", (lambda e: e.tensor_copy(rsc[0:64, :], pvB[0:64, :])), waits=wl0, inc=sV)
                ev_stag = ev(sV)
                op("tensor", (lambda e: e.matmul(pvB[0:64, :], onesb[0:1, 0:64], recl[0:1, 0:QG],
                                                 start=True, stop=True)), waits=(ev_recl, ev_stag))
                op("tensor", (lambda e: e.matmul(pvB[64:128, :], onesb[0:1, 0:64], recl[0:1, QG:2 * QG],
                                                 start=True, stop=True, tile_position=(0, 64))))
                op("tensor", (lambda e: e.matmul(pvA[64:128, :], id64b[:, :], rsc[0:64, :],
                                                 start=True, stop=True, tile_position=(0, 64))),
                   waits=((ev_id64,) if (qg == 0 and c == 0) else ()), inc=sT)
                ev_mv = ev(sT)
                op("vector", (lambda e: e.tensor_copy(repsb[:, :], pvB[:, :])), waits=(ev_mv,))
                odstA = OTg[0:64, QG * c: QG * c + QG]
                op("vector", (lambda e, o=odstA: e.tensor_tensor(o, pvA[0:64, :], repsb[0:64, :], ALU.mult)))
                odstB = OTg[64:128, QG * c: QG * c + QG]
                op("vector", (lambda e, o=odstB: e.tensor_tensor(o, pvA[64:128, :], repsb[64:128, :], ALU.mult)),
                   inc=sV)
                ev_evac_done[c % 2] = ev(sV)
            return emit_evac
        pend_evac_ev = [None]
        for c in range(NC):
            pvA, pvB = pvsets[c % 2]
            ev_sc = [None] * NT
            for kt in range(NT):
                psc = scp[0] if (kt % 2 == 0) else scp[1]
                w = []
                if kt == 0 and c == 0:
                    w.append(ev_qT_all)
                    if qg == 0:
                        w.append(ev_ph1_kT)
                if kt == 1 and c == 0 and qg > 0:
                    w.append(ev_ln4_prevqg[2])
                if ev_exp.get((qg, c, kt - 2)) is not None:
                    w.append(ev_exp[(qg, c, kt - 2)])
                if kt < 2 and c > 0:
                    w.append(ev_exp[(qg, c - 1, NT - 2 + kt)])
                lA = U1[0:64, T * c + 128 * kt: T * c + 128 * kt + 128]
                rA = qTg[0:64, QG * c: QG * c + QG]
                lB = U1[64:128, T * c + 128 * kt: T * c + 128 * kt + 128]
                rB = qTg[64:128, QG * c: QG * c + QG]
                op("tensor", (lambda e, o=psc[:, 0:512], l=lA, r=rA:
                              e.matmul(o, l, r, start=True, stop=True)), waits=tuple(w))
                op("tensor", (lambda e, o=psc[:, 512:1024], l=lB, r=rB:
                              e.matmul(o, l, r, start=True, stop=True, tile_position=(64, 0))), inc=sT)
                ev_sc[kt] = ev(sT)
                ptv = PT[:, 1024 * (kt % 2):1024 * (kt % 2) + 1024]
                we = [ev_sc[kt]]
                if qg > 0 and c == 0 and kt == 0:
                    we.append(ev_out_store_prevqg[1])
                op("scalar", (lambda e, o=ptv, i=psc[:, 0:1024]:
                              e.activation(o, i, AF.Exp, scale=0.125)), waits=tuple(we), inc=sS)
                ev_exp[(qg, c, kt)] = ev(sS)
                if pend_pv is not None:
                    pend_pv(); pend_pv = None
                if kt == 6 and pend_evac[0] is not None:
                    pend_evac[0](); pend_evac[0] = None
                def make_pv(c=c, kt=kt, qg=qg, pvA=pvA, pvB=pvB):
                    def emit_pv():
                        w = [ev_exp[(qg, c, kt)]]
                        if kt == 0:
                            if qg == 0 and c == 0:
                                w.append(ev_ph1_v)
                            if ev_evac_done[c % 2] is not None:
                                w.append(ev_evac_done[c % 2])
                            if qg > 0 and c < 2:
                                w.append(ev_ln4_prevqg[1 if c == 0 else 3])
                        lA65 = U2[:, 1040 * kt + 65 * (2 * c): 1040 * kt + 65 * (2 * c) + 65]
                        lB65 = U2[:, 1040 * kt + 65 * (2 * c + 1): 1040 * kt + 65 * (2 * c + 1) + 65]
                        ptv2 = PT[:, 1024 * (kt % 2):1024 * (kt % 2) + 1024]
                        op("tensor", (lambda e, o=pvA[0:65, :], l=lA65, r=ptv2[:, 0:512], st=(kt == 0), sp=(kt == NT - 1):
                                      e.matmul(o, l, r, start=st, stop=sp)), waits=tuple(w))
                        op("tensor", (lambda e, o=pvB[0:65, :], l=lB65, r=ptv2[:, 512:1024], st=(kt == 0), sp=(kt == NT - 1):
                                      e.matmul(o, l, r, start=st, stop=sp)),
                           inc=(sT if kt == NT - 1 else None))
                    return emit_pv
                pend_pv = make_pv()
            pend_pv(); pend_pv = None
            pend_evac_ev[0] = ev(sT)
            pend_evac[0] = make_evac(c, qg)
        ev_evac6 = ev_evac_done[(NC - 2) % 2]
        ev_scores_done_prev = ev_exp[(qg, NC - 1, NT - 1)]
        ev_OT_done = None
        def flush_evac7():
            nonlocal ev_OT_done
            pend_evac[0](); pend_evac[0] = None
            ev_OT_done = ev_evac_done[(NC - 1) % 2]
            if "ot" in dbg and qg == 0:
                op("gpsimd", lambda e: e.dma_start(out=dbg["ot"][:], in_=OTg[:]), waits=(ev_OT_done,), inc=sX, inc_by=16)

        ev_ln2_apply = [None] * TCQ
        ev_ln4_apply = [None] * TCQ
        st_state = {}; st2_state = {}; ev_xd = {}
        def xload(tc, wl, qg=qg):
            g = qg * TCQ + tc
            xv = xs[:, 1024 * (tc % 2):1024 * (tc % 2) + 1024]
            op("sync", (lambda e, o=xv, i=x_h.rearrange("(g p) c -> g p c", p=128)[g]:
                        e.dma_start(out=o, in_=i)), waits=tuple(wl), inc=sD, inc_by=16)
            ev_xd[tc] = ev(sD)
            return xv
        def tail_A(tc, mid=None, qg=qg):
            if tc < 2:
                xv = xload(tc, [ev_x1_last_prev] if qg > 0 else [ev_h[NT - 2 + tc]])
            else:
                xv = xs[:, 1024 * (tc % 2):1024 * (tc % 2) + 1024]
            psO = (S0, S1, P0, S0)[tc]
            tmp4 = (hs[:, 0:1024], PT[:, 0:1024], rop[:, 0:1024], OTg[:, 0:1024])[tc]
            for ng in range(2):
                pss = psO[:, 512 * ng:512 * ng + 512]
                for kc in range(NC):
                    if ng == 0 and kc == NC - 1 and mid is not None:
                        mid()
                    lhsT = OTg[:, QG * kc + 128 * tc: QG * kc + 128 * tc + 128]
                    rhs = W1[:, 1024 * kc + 512 * ng: 1024 * kc + 512 * ng + 512]
                    w = []
                    if kc == 0 and ng == 0:
                        w = [ev_evac6, ev_wo]
                        if tc < 2:
                            w.append(ev_scores_done_prev)
                        if tc == 3:
                            w.append(ev_ln4_apply[0])
                    if kc == NC - 1:
                        w.append(ev_OT_done)
                    op("tensor", (lambda e, o=pss, l=lhsT, r=rhs, st=(kc == 0), sp=(kc == NC - 1):
                                  e.matmul(o, l, r, start=st, stop=sp)),
                       waits=tuple(w), inc=(sT if (kc == NC - 1 and ng == 1) else None))
            ev_oproj = ev(sT)
            x1v = x1g[:, 1024 * tc:1024 * tc + 1024]
            st_state[tc] = (ev_oproj, x1v, xv, psO, tmp4)
        def tail_B1(tc, rope_job=None, qg=qg):
            nonlocal ev_ffn_prev, ev_oproj_last_prev
            ev_oproj, x1v, xv, psO, tmp4 = st_state[tc]
            if tc == TCQ - 1:
                ev_oproj_last_prev = ev_oproj
            evr2, negm2, rstd2, nmr2 = ln_stats(psO[:, 0:1024], (ev_oproj,))
            op("scalar", (lambda e, s=psO, b=nmr2, rs=rstd2, t=tmp4:
                          e.activation(t, s[:, 0:1024], AF.Identity, bias=b, scale=rs)),
               waits=(evr2,), inc=sS)
            ev_ln2_apply[tc] = ev(sS)
            wl2 = [ev_ln2_apply[tc], ev_xd[tc]]
            if qg > 0:
                wl2.append(ev_out_store_prevqg[tc])
            op("vector", (lambda e, o=x1v, i=xv, t=tmp4: e.tensor_tensor(o, i, t, ALU.add)),
               waits=tuple(wl2), inc=sV)
            ev_x1_done[tc] = ev(sV)
            if tc < 2:
                xload(tc + 2, [ev_x1_done[tc]])
            evr3, negm3, rstd3, nmr3 = ln_stats(x1v, (ev_x1_done[tc],))
            wl = [evr3] + ([ev_ffn_prev] if ev_ffn_prev is not None else [])
            op("scalar", (lambda e, i=x1v, b=nmr3, rs=rstd3:
                          e.activation(h2s[:, 0:512], i[:, 0:512], AF.Identity, bias=b, scale=rs)),
               waits=tuple(wl), inc=sS)
            ev_h2a = ev(sS)
            op("scalar", (lambda e, i=x1v, b=nmr3, rs=rstd3:
                          e.activation(h2s[:, 512:1024], i[:, 512:1024], AF.Identity, bias=b, scale=rs)),
               inc=sS)
            ev_h2b = ev(sS)
            ev_h2Ta = transpose_tile_ap(h2TgT[:, 0:512], 128, 0, h2s[:, 0:512], (ev_h2a,))
            ev_h2T = transpose_tile_ap(h2TgT[:, 512:1024], 128, 0, h2s[:, 512:1024], (ev_h2b,))
            if rope_job is not None:
                rope_job()
            psY = psO
            for ng in range(2):
                pss = psY[:, 512 * ng:512 * ng + 512]
                for kc in range(NC):
                    lhsT = h2TgT[:, 128 * kc:128 * kc + 128]
                    rhs = Msb[:, 1024 * kc + 512 * ng: 1024 * kc + 512 * ng + 512]
                    w = []
                    if kc == 0 and ng == 0:
                        w = [ev_h2Ta, ev_m_done, ev_bias]
                    if kc == 4 and ng == 0:
                        w = [ev_h2T]
                    op("tensor", (lambda e, o=pss, l=lhsT, r=rhs, st=(kc == 0):
                                  e.matmul(o, l, r, start=st, stop=False)),
                       waits=tuple(w))
                op("tensor", (lambda e, o=pss, r=biasb[0:1, 512 * ng:512 * ng + 512]:
                              e.matmul(o, onesb[0:1, 0:128], r, start=False, stop=True)),
                   inc=(sT if ng == 1 else None))
            ev_ffn = ev(sT)
            ev_ffn_prev = ev_ffn
            st2_state[tc] = ev_ffn
        def tail_B2a(tc, rope_job=None, qg=qg):
            ev_oproj, x1v, xv, psY, tmp4 = st_state[tc]
            ev_ffn = st2_state[tc]
            evr4, negm4, rstd4, nmr4 = ln_stats(psY[:, 0:1024], (ev_ffn,))
            if rope_job is not None:
                rope_job()
            wl4 = [evr4]
            if tc == 2 and qg + 1 < NQG:
                wl4.append(qp_qT[qg + 1][2])
            op("scalar", (lambda e, s=psY, b=nmr4, rs=rstd4, t=tmp4:
                          e.activation(t, s[:, 0:1024], AF.Identity, bias=b, scale=rs)),
               waits=tuple(wl4), inc=sS)
            ev_ln4_apply[tc] = ev(sS)
        def tail_B2b(tc, qg=qg):
            nonlocal ev_out_store_prev
            g = qg * TCQ + tc
            ev_oproj, x1v, xv, psY, tmp4 = st_state[tc]
            op("gpsimd", (lambda e, i=x1v, t=tmp4: e.tensor_tensor(i, i, t, ALU.add)),
               waits=(ev_ln4_apply[tc],), inc=sP)
            ev_o = ev(sP)
            op("gpsimd", (lambda e, i=x1v, o=out_h.rearrange("(g p) c -> g p c", p=128)[g]:
                        e.dma_start(out=o, in_=i)), waits=(ev_o,), inc=sX, inc_by=16)
            ev_out_store[tc] = ev(sX)
            ev_out_store_prev = ev(sX)
        ev_out_store = ev_out_store if qg > 0 else [None] * TCQ
        nq = qg + 1 if qg + 1 < NQG else None
        tail_A(0, mid=flush_evac7); tail_A(1); tail_A(2)
        if nq is not None: emit_qp_proj(nq, 0, ev_OT_done)
        tail_B1(0, (lambda: emit_qp_rope(nq, 0)) if nq is not None else None)
        if nq is not None: emit_qp_proj(nq, 1, ev_OT_done)
        tail_B1(1)
        tail_B2a(0, (lambda: emit_qp_rope(nq, 1)) if nq is not None else None)
        if nq is not None: emit_qp_proj(nq, 2, ev_OT_done)
        tail_A(3)
        tail_B1(2)
        tail_B2a(1, (lambda: emit_qp_rope(nq, 2)) if nq is not None else None)
        tail_B2b(0)
        if nq is not None: emit_qp_proj(nq, 3, ev_OT_done)
        tail_B1(3)
        tail_B2a(2, (lambda: emit_qp_rope(nq, 3)) if nq is not None else None)
        tail_B2b(1)
        tail_B2a(3)
        tail_B2b(2)
        tail_B2b(3)
        ev_ln4_prevqg = ev_ln4_apply
        if "x1" in dbg and qg == 0:
            op("sync", lambda e: e.dma_start(out=dbg["x1"][:], in_=x1g[:]), waits=(ev_x1_done[TCQ - 1],), inc=sX, inc_by=16)
        ev_x1_last_prev = ev_x1_done[TCQ - 1]
        ev_out_store_last_prev = ev_out_store[TCQ - 1]
        ev_out_store_prevqg = list(ev_out_store)

    ops["sync"].append(lambda e: e.wait_ge(sX.h, sX.n))

    blk = E(nc.Block())
    @blk.sync
    def _(e):
        for f in ops["sync"]: f(e)
    @blk.gpsimd
    def _(e):
        for f in ops["gpsimd"]: f(e)
    @blk.tensor
    def _(e):
        for f in ops["tensor"]: f(e)
    @blk.vector
    def _(e):
        for f in ops["vector"]: f(e)
    @blk.scalar
    def _(e):
        for f in ops["scalar"]: f(e)
    es.close()
    return nc, dbg

def make_inmaps(ins):
    Cp, Sp, cosTt, sinTt, ones128, ident64 = host_constants()
    f = perm_f_of_pos()
    base = dict(
        wqpt=np.ascontiguousarray(ins["L0_attn_Wq"][f].T),
        wkpt=np.ascontiguousarray(ins["L0_attn_Wk"][f].T),
        wvt=np.ascontiguousarray(ins["L0_attn_Wv"].T),
        wot=np.ascontiguousarray(ins["L0_attn_Wo"].T),
        cp=Cp, sp=Sp,
        wrt=np.ascontiguousarray(ins["L0_ffn_w_freq_real"].T),
        wit=np.ascontiguousarray(ins["L0_ffn_w_freq_imag"].T),
        biasrow=np.ascontiguousarray(ins["L0_ffn_bias"].reshape(1, D)),
        cost=cosTt, sint=sinTt, ones128=ones128, ident64=ident64)
    x = ins["L0_attn_x"]
    return [dict(base, x=np.ascontiguousarray(x[b])) for b in range(x.shape[0])]

_CACHE = {}

def kernel(L0_attn_x, L0_attn_Wq, L0_attn_Wk, L0_attn_Wv, L0_attn_Wo,
           L0_ffn_w_freq_real, L0_ffn_w_freq_imag, L0_ffn_bias):
    ins = dict(L0_attn_x=np.asarray(L0_attn_x), L0_attn_Wq=np.asarray(L0_attn_Wq),
               L0_attn_Wk=np.asarray(L0_attn_Wk), L0_attn_Wv=np.asarray(L0_attn_Wv),
               L0_attn_Wo=np.asarray(L0_attn_Wo),
               L0_ffn_w_freq_real=np.asarray(L0_ffn_w_freq_real),
               L0_ffn_w_freq_imag=np.asarray(L0_ffn_w_freq_imag),
               L0_ffn_bias=np.asarray(L0_ffn_bias))
    if "nc" not in _CACHE:
        _CACHE["nc"] = build()[0]
    nc = _CACHE["nc"]
    in_maps = make_inmaps(ins)
    res = run_bass_kernel_spmd(nc, in_maps, core_ids=list(range(8)))
    out = np.stack([np.asarray(r["out"]) for r in res.results], axis=0)
    return out.astype(np.float32)
